# version 01 (final)
# baseline (speedup 1.0000x reference)
import os
import numpy as np
import ml_dtypes
from contextlib import ExitStack

KCUT = int(os.environ.get("KCUT", "99"))
KDBG = os.environ.get("KDBG", "")
KQC = int(os.environ.get("KQC", "0"))
KCUT_FFN = int(os.environ.get("KCUT_FFN", "99"))
KCUTG = int(os.environ.get("KCUTG", "99"))

import concourse.bass as bass
import concourse.mybir as mybir
from concourse.bass_utils import run_bass_kernel_spmd

F32 = mybir.dt.float32
BF16 = mybir.dt.bfloat16
AF = mybir.ActivationFunctionType
ALU = mybir.AluOpType

NCORE = 8
D = 1024
FC = 8
T = 2048
NSLOT = 16
NB = 2
SEQ = 8192
NEXP = 8
H2 = 2048
NHC = 16
LN_EPS = 1e-5

ENGINES = ("tensor", "vector", "scalar", "gpsimd", "sync")
ALL_PHASES = ("expm", "attn_a", "ffn1", "attn_b", "ffn2")

GRP4 = [[0, 1, 2, 3], [4, 5, 6, 7]]
GRP8 = [[0, 1, 2, 3, 4, 5, 6, 7]]

FACT = [1.0]
for _k in range(1, 17):
    FACT.append(FACT[-1] / _k)


class Arena:

    def __init__(self, A, cols):
        self.A = A
        self.cols = cols
        self.off = 0

    def bf(self, shape):
        n = int(np.prod(shape))
        assert self.off + n <= self.cols, f"arena overflow {self.off}+{n}"
        v = self.A[:, self.off:self.off + n]
        self.last_off = self.off
        self.off += n + (-n) % 2
        if len(shape) > 1:
            dims = " ".join(f"d{i}" for i in range(len(shape)))
            v = v.rearrange(f"p ({dims}) -> p {dims}",
                            **{f"d{i}": s for i, s in enumerate(shape)})
        return v

    def f32(self, shape):
        n = int(np.prod(shape))
        assert self.off % 2 == 0
        assert self.off + 2 * n <= self.cols, f"arena overflow f32 {self.off}+{2*n}"
        v = self.A[:, self.off:self.off + 2 * n].bitcast(F32)
        self.off += 2 * n
        if len(shape) > 1:
            dims = " ".join(f"d{i}" for i in range(len(shape)))
            v = v.rearrange(f"p ({dims}) -> p {dims}",
                            **{f"d{i}": s for i, s in enumerate(shape)})
        return v

    def mark(self):
        return self.off

    def reset(self, m):
        self.off = m


class K:

    def __init__(self, phases):
        self.phases = tuple(phases)
        self.nc = bass.Bass("TRN2", target_bir_lowering=False, debug=False,
                            detect_race_conditions=False)
        self.q = {e: [] for e in ENGINES}
        self.cnt = {}
        self.inp = {}

    def emit(self, eng, fn, wait=(), inc=()):
        for s, n in inc:
            self.cnt[s] += n
        waits = [(self.sems[s], v) for s, v in wait]
        incs = [(self.sems[s], n) for s, n in inc]

        def run(e):
            for sh, v in waits:
                e.wait_ge(sh, v)
            r = fn(e)
            if incs:
                if r is None:
                    r = e.drain()
                for sh, n in incs:
                    r = r.then_inc(sh, n)
        self.q[eng].append(run)

    def c(self, name):
        return self.cnt[name]

    def wait(self, eng, *specs):
        w = [(s, self.cnt[s]) if isinstance(s, str) else s for s in specs]
        self.emit(eng, lambda e: None, wait=w)

    def din(self, name, shape, dtype=F32):
        self.inp[name] = self.nc.dram_tensor(
            name, list(shape), dtype, kind="ExternalInput").ap()
        return self.inp[name]

    def build(self):
        nc = self.nc
        i = self.din
        i("x_fm", [128, FC, T])
        i("wq_a", [128, FC, 64])
        i("wkv_a", [128, FC, 128])
        i("wout_a", [64, FC, 128])
        i("bout_a", [128, FC])
        i("wq_b", [128, FC, D])
        i("wk_b", [128, FC, D])
        i("wv_b", [128, FC, D])
        i("wo_b", [128, FC, D])
        i("bq_b", [128, FC])
        i("bk_b", [128, FC])
        i("bv_fm", [128, FC])
        i("bo_b", [128, FC])
        i("ln_g", [128, FC])
        i("ln_b", [128, FC])
        i("abel_w", [128, FC, D])
        i("abel_b", [128, FC])
        i("gate_w", [128, FC, 64])
        i("gate_b", [128, NEXP])
        i("ew1", [NEXP, NHC, 128, FC, 128])
        i("eb1", [128, NEXP * NHC])
        i("ew2", [NEXP, NHC, 128, D])
        i("eb2T", [128, FC, NEXP])
        i("gcw1", [128, FC])
        i("gcw2", [128, FC])
        i("gcb", [128, 1])
        i("gens", [3, 128, FC, D])
        i("gensT", [3, 128, FC, 128])
        i("diag_off", [128, 1])
        i("eye_ind", [128, FC])

        dt = nc.dram_tensor
        self.out_fm = dt("out_fm", [128, FC, T], F32, kind="ExternalOutput").ap()
        self.dbgf = dt("dbgf", [128, 4096], F32, kind="ExternalOutput").ap()
        self.dbgb = dt("dbgb", [128, 8192], BF16, kind="ExternalOutput").ap()

        ap = lambda t: t.ap()
        self.kdr = ap(dt("kdr", [NSLOT, 128, FC, 128], BF16))
        self.vdr = ap(dt("vdr", [NSLOT, 128, D], BF16))
        self.nrm = ap(dt("nrm", [1, 16], F32))
        self.kg = ap(dt("kg", [4, NSLOT, 128, FC, 128], BF16))
        self.vg = ap(dt("vg", [4, NSLOT, 128, D], BF16))
        self.nrmg = ap(dt("nrmg", [4, 16], F32))
        self.ctx_in = ap(dt("ctx_in", [64, 64], F32))
        self.ctx_out = ap(dt("ctx_out", [64, 64], F32))
        self.gpiece = [ap(dt(f"gpiece{i_}", [128, D], BF16)) for i_ in range(9)]
        self.gfull = [ap(dt(f"gfull{i_}", [8, 128, D], BF16)) for i_ in range(9)]
        self.evst = ap(dt("evst", [128, FC, 512], F32))
        self.ewc1 = ap(dt("ewc1", [NEXP, NHC, 128, FC, 128], BF16))
        self.ewc2 = ap(dt("ewc2", [NEXP, NHC, 128, D], BF16))

        with ExitStack() as st:
            e_ = st.enter_context
            sem_names = (
                ["T", "V", "S", "G", "Y", "dX", "dC", "dCg", "dE", "cc", "dM", "dSt", "dB"]
                + [f"dW1_{k}" for k in range(3)]
                + [f"dW2_{k}" for k in range(5)]
                + ["dK0", "dK1", "dV0", "dV1"]
            )
            self.sems = {n: e_(nc.semaphore(n)) for n in sem_names}
            self.cnt = {n: 0 for n in sem_names}

            sb = nc.sbuf_tensor
            self.stream = e_(sb("stream", [128, FC, T], F32))
            self.R_sb = e_(sb("R_sb", [128, FC, D], BF16))
            self.eb2R = e_(sb("eb2R", [NEXP, D], BF16))
            self.ones_all = e_(sb("ones_all", [128, 128], BF16))
            self.ones_col = e_(sb("ones_col", [1, 128], BF16))
            self.ident = e_(sb("ident", [128, 128], BF16))
            self.ident_f = e_(sb("ident_f", [128, 128], F32))
            self.cmask = e_(sb("cmask", [128, 512], F32))
            self.blkm = e_(sb("blkm", [64, 64], F32))
            self.iotaq = e_(sb("iotaq", [128, 512], F32))
            for nm in ("bout_a", "bq_b", "bk_b", "bv_fm", "bo_b", "ln_g",
                       "ln_b", "abel_b", "gcw1", "gcw2", "diag_off",
                       "eye_ind"):
                setattr(self, "c_" + nm, e_(sb("c_" + nm, [128, FC if nm != "diag_off" else 1], F32)))
            self.c_eb1 = e_(sb("c_eb1", [128, NEXP * NHC], F32))
            self.c_eb2T = e_(sb("c_eb2T", [128, FC, NEXP], BF16))
            self.c_gatew = e_(sb("c_gatew", [128, FC, 64], BF16))
            self.c_gb = e_(sb("c_gb", [128, NEXP], F32))
            self.c_gcb = e_(sb("c_gcb", [128, 1], F32))
            self.c_obias = e_(sb("c_obias", [128, FC], F32))
            self.smx = e_(sb("smx", [128, 128], F32))
            self.smy = e_(sb("smy", [128, 96], F32))
            self.wqa = e_(sb("wqa", [128, FC, 64], BF16))
            self.wkva = e_(sb("wkva", [128, FC, 128], BF16))
            self.wouta = e_(sb("wouta", [64, FC, 128], BF16))

            AB_COLS = 57536
            self.A = e_(sb("A", [128, AB_COLS], BF16))
            self.ar = Arena(self.A, AB_COLS)

            self.pA = e_(nc.psum_tensor("pA", [128, 512], F32))
            self.pB = e_(nc.psum_tensor("pB", [128, 512], F32))
            self.pC = e_(nc.psum_tensor("pC", [128, 1024], F32))
            self.pD = e_(nc.psum_tensor("pD", [128, 1024], F32))
            self.pT = e_(nc.psum_tensor("pT", [128, 512], BF16))
            self.pS = e_(nc.psum_tensor("pS", [128, 128], F32))

            self.prologue()
            if "expm" in self.phases:
                self.expm()
            if "attn_a" in self.phases:
                self.attn_a()
            if "ffn1" in self.phases:
                self.ffn(0)
            if "attn_b" in self.phases:
                self.attn_b()
            if "ffn2" in self.phases:
                self.ffn(1)
            self.epilogue()

            with nc.Block() as block:
                @block.tensor
                def _(t):
                    for f in self.q["tensor"]:
                        f(t)

                @block.vector
                def _(v):
                    for f in self.q["vector"]:
                        f(v)

                @block.scalar
                def _(s):
                    for f in self.q["scalar"]:
                        f(s)

                @block.gpsimd
                def _(g):
                    for f in self.q["gpsimd"]:
                        f(g)

                @block.sync
                def _(sy):
                    for f in self.q["sync"]:
                        f(sy)
        return nc

    def dump(self, tag, ap_, wait_engine="vector"):
        if KDBG != tag:
            return
        en = {"vector": "V", "scalar": "S", "tensor": "T", "gpsimd": "G"}[wait_engine]
        is_f32 = ap_.dtype == F32
        dst = self.dbgf if is_f32 else self.dbgb
        n = int(np.prod(ap_.shape[1:]))
        npart = ap_.shape[0]
        dv = dst[0:npart, 0:n]
        if len(ap_.shape) == 3:
            dv = dv.rearrange("p (a b) -> p a b", a=ap_.shape[1], b=ap_.shape[2])
        self.emit("sync", lambda e: e.dma_start(
            out=dv, in_=ap_),
            wait=[(en, self.c(en)), ("T", self.c("T")), ("S", self.c("S")),
                  ("V", self.c("V")), ("dM", self.c("dM")), ("G", self.c("G")),
                  ("cc", self.c("cc"))],
            inc=[("dE", 16)])
        for eng in ("vector", "scalar", "tensor", "gpsimd"):
            self.wait(eng, ("dE", self.c("dE")))

    def prologue(self):
        inp = self.inp
        self.emit("sync", lambda e: e.dma_start(out=self.stream[:], in_=inp["x_fm"]),
                  inc=[("dX", 16)])
        for nm in ("bout_a", "bq_b", "bk_b", "bv_fm", "bo_b", "ln_g", "ln_b",
                   "abel_b", "gcw1", "gcw2", "diag_off", "eye_ind"):
            dst = getattr(self, "c_" + nm)
            self.emit("sync", lambda e, d=dst, s=inp[nm]: e.dma_start(out=d[:], in_=s),
                      inc=[("dC", 16)])
        self.emit("sync", lambda e: e.dma_start(out=self.c_eb1[:], in_=inp["eb1"]),
                  inc=[("dC", 16)])
        for dst, src in ((self.c_eb2T, "eb2T"), (self.c_gatew, "gate_w"),
                         (self.wqa, "wq_a"), (self.wkva, "wkv_a"),
                         (self.wouta, "wout_a")):
            self.emit("gpsimd", lambda e, d=dst, s=inp[src]: e.dma_start(out=d[:], in_=s),
                      inc=[("dCg", 16)])
        self.emit("sync", lambda e: e.dma_start(out=self.c_gb[:], in_=inp["gate_b"]),
                  inc=[("dC", 16)])
        self.emit("sync", lambda e: e.dma_start(out=self.c_gcb[:], in_=inp["gcb"]),
                  inc=[("dC", 16)])

        self.constv = [LN_EPS, 1.0 / 32.0, 0.0, -1.0 / 64.0]
        self.c_const = self.ar.f32([len(self.constv)])
        for k, val in enumerate(self.constv):
            self.nc.const_aps.register(F32, val, self.c_const[:, k:k + 1])

        def consts(g):
            g.memset(self.ones_all[:], 1.0)
            g.memset(self.ones_col[:], 1.0)
            g.memset(self.ident[:], 0.0)
            g.affine_select(out=self.ident[:], in_=self.ident[:],
                            compare_op=ALU.not_equal, fill=1.0, base=0,
                            pattern=[[-1, 128]], channel_multiplier=1)
            g.memset(self.ident_f[:], 0.0)
            g.affine_select(out=self.ident_f[:], in_=self.ident_f[:],
                            compare_op=ALU.not_equal, fill=1.0, base=0,
                            pattern=[[-1, 128]], channel_multiplier=1)
            g.iota(self.iotaq[:], pattern=[[1, 512]], base=0,
                   channel_multiplier=-1, allow_small_or_imprecise_dtypes=True)
            g.iota(self.blkm[:], pattern=[[-8, 8], [0, 8]], base=0,
                   channel_multiplier=1, allow_small_or_imprecise_dtypes=True)
            for k, val in enumerate(self.constv):
                g.memset(self.c_const[:, k:k + 1], val)
            return None
        self.emit("gpsimd", consts, inc=[("G", 1)])

        def masks(v):
            v.tensor_scalar(self.cmask[0:64, 0:64], self.blkm[:], 0.0, None,
                            ALU.is_ge)
            v.tensor_scalar(self.blkm[:], self.blkm[:], 7.0, None, ALU.is_le)
            v.tensor_tensor(self.blkm[:], self.blkm[:], self.cmask[0:64, 0:64],
                            ALU.mult)
            v.tensor_scalar(self.blkm[:], self.blkm[:], 0.125, None, ALU.mult)
            v.tensor_scalar(self.cmask[:], self.iotaq[:],
                            self.c_diag_off[:, 0:1], None, ALU.subtract)
            v.tensor_scalar(self.cmask[:], self.cmask[:], 0.0, None, ALU.is_le)
            v.tensor_scalar(self.cmask[:], self.cmask[:], 1e9, -1e9,
                            ALU.mult, ALU.add)
            return None
        self.emit("vector", masks,
                  wait=[("G", self.c("G")), ("dC", self.c("dC")),
                        ("dCg", self.c("dCg"))],
                  inc=[("V", 1)])
        onesf = self.ar.f32([128])
        self.onesf = onesf
        self.emit("vector", lambda e: e.memset(onesf[0:1, :], 1.0), inc=[("V", 1)])
        for eng in ENGINES:
            self.wait(eng, "G", "dC", "dCg")
        self.wait("sync", "dX")

    def ln_tile(self, src_fc, out_fc, n, gamma=None, beta=None, src_bf=False,
                save_stats=None):
        xb, xq = self.ln_xb, self.ln_xq
        mu, var, isg = self.ln_mu, self.ln_var, self.ln_isg
        assert n <= 512
        if not src_bf:
            for fc in range(FC):
                self.emit("vector", lambda e, fc=fc: e.tensor_copy(
                    xb[:, fc, 0:n], src_fc(fc)))
            self.emit("vector", lambda e: None, inc=[("V", 1)])
        for fc in range(FC):
            self.emit("scalar", lambda e, fc=fc: e.activation(
                xq[:, fc, 0:n], src_fc(fc), AF.Square),
                wait=([("V", self.c("V"))] if fc == 0 else []))
        self.emit("scalar", lambda e: None, inc=[("S", 1)])
        wv, ws = self.c("V"), self.c("S")
        for fc in range(FC):
            s_ap = src_fc(fc) if src_bf else xb[:, fc, 0:n]
            self.emit("tensor", lambda e, fc=fc, s_ap=s_ap: e.matmul(
                self.pA[:, 0:n], self.ones_all[:], s_ap,
                start=(fc == 0), stop=(fc == FC - 1)),
                wait=([("V", wv)] if fc == 0 else []))
        for fc in range(FC):
            self.emit("tensor", lambda e, fc=fc: e.matmul(
                self.pB[:, 0:n], self.ones_all[:], xq[:, fc, 0:n],
                start=(fc == 0), stop=(fc == FC - 1)),
                wait=([("S", ws)] if fc == 0 else []))
        self.emit("tensor", lambda e: None, inc=[("T", 1)])
        wt = self.c("T")
        self.emit("vector", lambda e: e.tensor_scalar(
            mu[:, 0:n], self.pA[:, 0:n], 1.0 / D, None, ALU.mult),
            wait=[("T", wt)])
        self.emit("vector", lambda e: e.tensor_tensor(
            var[:, 0:n], mu[:, 0:n], mu[:, 0:n], ALU.mult))
        self.emit("vector", lambda e: e.scalar_tensor_tensor(
            var[:, 0:n], self.pB[:, 0:n], 1.0 / D, var[:, 0:n],
            ALU.mult, ALU.subtract), inc=[("V", 1)])
        self.emit("scalar", lambda e: e.activation(
            isg[:, 0:n], var[:, 0:n], AF.Sqrt, bias=LN_EPS),
            wait=[("V", self.c("V"))], inc=[("S", 1)])
        self.emit("vector", lambda e: e.drain())
        self.emit("vector", lambda e: e.reciprocal(isg[:, 0:n], isg[:, 0:n]),
                  wait=[("S", self.c("S"))])
        self.emit("vector", lambda e: e.drain())
        if save_stats is not None:
            mu_s, is_s = save_stats
            self.emit("vector", lambda e: e.tensor_copy(mu_s, mu[:, 0:n]))
            self.emit("vector", lambda e: e.tensor_copy(is_s, isg[:, 0:n]))
        for fc in range(FC):
            self.emit("vector", lambda e, fc=fc: e.tensor_tensor(
                var[:, 0:n], src_fc(fc), mu[:, 0:n], ALU.subtract))
            if gamma is None:
                self.emit("vector", lambda e, fc=fc: e.tensor_tensor(
                    out_fc(fc), var[:, 0:n], isg[:, 0:n], ALU.mult))
            else:
                self.emit("vector", lambda e, fc=fc: e.tensor_tensor(
                    var[:, 0:n], var[:, 0:n], isg[:, 0:n], ALU.mult))
                self.emit("vector", lambda e, fc=fc: e.tensor_scalar(
                    out_fc(fc), var[:, 0:n], gamma[:, fc:fc + 1],
                    beta[:, fc:fc + 1], ALU.mult, ALU.add))
        self.emit("vector", lambda e: None, inc=[("V", 1)])

    def alloc_ln_scratch(self):
        self.ln_xb = self.ar.bf([FC, 512])
        self.ln_xq = self.ar.bf([FC, 512])
        self.ln_mu = self.ar.f32([512])
        self.ln_var = self.ar.f32([512])
        self.ln_isg = self.ar.f32([512])

    def wprod(self, lfull, rhs_w, out_fc, psum, evac):
        for mc in range(FC):
            for kc in range(FC):
                self.emit("tensor", lambda e, mc=mc, kc=kc: e.matmul(
                    psum[:, 0:128], lfull[:, kc, mc * 128:(mc + 1) * 128],
                    rhs_w(kc), start=(kc == 0), stop=(kc == FC - 1)),
                    wait=([("V", self.c("V"))] if kc == 0 else []))
            self.emit("tensor", lambda e: None, inc=[("T", 1)])
            evac(mc, psum[:, 0:128])

    def gather_full(self, w_fc, dest, gi):
        trp = self.trp
        for kc in range(FC):
            self.emit("tensor", lambda e, kc=kc: e.transpose(
                self.pT[:, 0:128], w_fc(kc), self.ident[:]),
                wait=[("V", self.c("V"))], inc=[("T", 1)])
            self.emit("vector", lambda e, kc=kc: e.tensor_copy(
                trp[:, kc * 128:(kc + 1) * 128], self.pT[:, 0:128]),
                wait=[("T", self.c("T"))], inc=[("V", 1)])
        if KDBG == "alltrp" and gi < 8:
            self.emit("sync", lambda e, gi=gi: e.dma_start(
                out=self.dbgb[:, gi * 1024:(gi + 1) * 1024], in_=trp[:]),
                wait=[("V", self.c("V"))], inc=[("dE", 16)])
        self.emit("gpsimd", lambda e, gi=gi: e.dma_start(
            out=self.gpiece[gi], in_=trp[:]),
            wait=[("V", self.c("V"))], inc=[("dM", 16)])
        self.emit("gpsimd", lambda e, gi=gi: e.collective_compute(
            "AllGather", ALU.bypass, replica_groups=GRP8,
            ins=[self.gpiece[gi].opt()], outs=[self.gfull[gi].opt()]),
            wait=[("dM", self.c("dM"))], inc=[("cc", 1)])
        self.emit("gpsimd", lambda e, gi=gi, dest=dest: e.dma_start(
            out=dest, in_=self.gfull[gi].rearrange("c p m -> p c m")),
            wait=[("cc", self.c("cc"))], inc=[("dM", 16)])
        self.wait("tensor", ("dM", self.c("dM")))
        self.wait("vector", ("dM", self.c("dM")))

    def expm(self):
        ar = self.ar
        m0 = ar.mark()
        afull = ar.bf([FC, D])
        a2full = ar.bf([FC, D])
        a4full = ar.bf([FC, D])
        wa = ar.bf([FC, 128])
        wa2 = ar.bf([FC, 128])
        wa3 = ar.bf([FC, 128])
        wa4 = ar.bf([FC, 128])
        qcur = ar.bf([FC, 128])
        qnxt = ar.bf([FC, 128])
        wr = ar.f32([FC, 128])
        wrt = ar.f32([128])
        self.trp = ar.bf([D])
        qe = [ar.bf([FC, 128]) for _ in range(3)]
        wm01 = ar.bf([FC, 128])
        gi = 0

        self.wait("gpsimd", "G", "dC", "dCg")
        for i in range(3):
            self.wait("gpsimd", ("T", self.c("T")), ("V", self.c("V")))
            self.emit("gpsimd", lambda e, i=i: e.dma_start(
                out=afull[:], in_=self.inp["gens"][i]), inc=[("dM", 16)])
            self.emit("gpsimd", lambda e, i=i: e.dma_start(
                out=wa[:], in_=self.inp["gensT"][i]), inc=[("dM", 16)])
            self.wait("tensor", ("dM", self.c("dM")))
            self.wait("vector", ("dM", self.c("dM")))
            self.wprod(afull, lambda kc: wa[:, kc, :], wa2, self.pA,
                       lambda mc, ps: self.emit("vector",
                           lambda e, mc=mc, ps=ps: e.tensor_copy(wa2[:, mc, :], ps),
                           wait=[("T", self.c("T"))], inc=[("V", 1)]))
            if i == 0:
                self.dump("wa2", wa2[:])
            self.gather_full(lambda kc: wa2[:, kc, :], a2full[:], gi); gi += 1
            if i == 0:
                self.dump("a2full", a2full[:, :, 0:512])
            self.wprod(afull, lambda kc: wa2[:, kc, :], wa3, self.pA,
                       lambda mc, ps: self.emit("vector",
                           lambda e, mc=mc, ps=ps: e.tensor_copy(wa3[:, mc, :], ps),
                           wait=[("T", self.c("T"))], inc=[("V", 1)]))
            self.wprod(a2full, lambda kc: wa2[:, kc, :], wa4, self.pA,
                       lambda mc, ps: self.emit("vector",
                           lambda e, mc=mc, ps=ps: e.tensor_copy(wa4[:, mc, :], ps),
                           wait=[("T", self.c("T"))], inc=[("V", 1)]))
            self.gather_full(lambda kc: wa4[:, kc, :], a4full[:], gi); gi += 1
            if i == 0:
                self.dump("wa4", wa4[:])
            for kc in range(FC):
                self.emit("vector", lambda e, kc=kc: e.tensor_scalar(
                    qcur[:, kc, :], self.ident[:],
                    self.c_eye_ind[:, kc:kc + 1], FACT[16], ALU.mult, ALU.mult))
            self.emit("vector", lambda e: None, inc=[("V", 1)])
            for j in (3, 2, 1, 0):
                c0, c1, c2, c3 = FACT[4 * j], FACT[4 * j + 1], FACT[4 * j + 2], FACT[4 * j + 3]
                for kc in range(FC):
                    self.emit("vector", lambda e, kc=kc, c2=c2: e.tensor_scalar(
                        wr[:, kc, :], wa2[:, kc, :], c2, None, ALU.mult))
                    self.emit("vector", lambda e, kc=kc, c1=c1: e.scalar_tensor_tensor(
                        wr[:, kc, :], wa[:, kc, :], c1, wr[:, kc, :],
                        ALU.mult, ALU.add))
                    self.emit("vector", lambda e, kc=kc, c3=c3: e.scalar_tensor_tensor(
                        wr[:, kc, :], wa3[:, kc, :], c3, wr[:, kc, :],
                        ALU.mult, ALU.add))
                    self.emit("vector", lambda e, kc=kc, c0=c0: e.tensor_scalar(
                        wrt[:], self.ident[:], self.c_eye_ind[:, kc:kc + 1],
                        c0, ALU.mult, ALU.mult))
                    self.emit("vector", lambda e, kc=kc: e.tensor_tensor(
                        wr[:, kc, :], wr[:, kc, :], wrt[:], ALU.add))
                self.emit("vector", lambda e: None, inc=[("V", 1)])
                qa, qb = (qcur, qnxt) if (3 - j) % 2 == 0 else (qnxt, qcur)
                self.wprod(a4full, lambda kc, qa=qa: qa[:, kc, :], qb, self.pA,
                           lambda mc, ps, qb=qb: self.emit("vector",
                               lambda e, mc=mc, ps=ps: e.tensor_add(
                                   qb[:, mc, :], ps, wr[:, mc, :]),
                               wait=[("T", self.c("T"))], inc=[("V", 1)]))
            self.dump(f"qe{i}", qcur[:])
            qfin = qcur if True else qnxt
            for kc in range(FC):
                self.emit("vector", lambda e, kc=kc, i=i: e.tensor_copy(
                    qe[i][:, kc, :], qcur[:, kc, :]))
            self.emit("vector", lambda e: None, inc=[("V", 1)])
        e1f = ar.bf([FC, D])
        self.dump("qe1late", qe[1][:])
        self.gather_dbg = True
        self.gather_full(lambda kc: qe[1][:, kc, :], e1f[:], gi); gi += 1
        self.gather_dbg = False
        self.dump("e1full", e1f[:])
        self.wprod(e1f, lambda kc: qe[0][:, kc, :], wm01, self.pA,
                   lambda mc, ps: self.emit("vector",
                       lambda e, mc=mc, ps=ps: e.tensor_copy(wm01[:, mc, :], ps),
                       wait=[("T", self.c("T"))], inc=[("V", 1)]))
        self.dump("wm01", wm01[:])
        self.gather_full(lambda kc: qe[2][:, kc, :], a4full[:], gi); gi += 1
        self.wprod(a4full, lambda kc: wm01[:, kc, :], qcur, self.pA,
                   lambda mc, ps: self.emit("vector",
                       lambda e, mc=mc, ps=ps: e.tensor_copy(qcur[:, mc, :], ps),
                       wait=[("T", self.c("T"))], inc=[("V", 1)]))
        self.dump("wR", qcur[:])
        self.gather_full(lambda kc: qcur[:, kc, :], self.R_sb[:], gi); gi += 1
        for kc in range(FC):
            for half in range(2):
                self.emit("tensor", lambda e, kc=kc, half=half: e.matmul(
                    self.pC[0:NEXP, half * 512:(half + 1) * 512],
                    self.c_eb2T[:, kc, :],
                    self.R_sb[:, kc, half * 512:(half + 1) * 512],
                    start=(kc == 0), stop=(kc == FC - 1)))
        self.emit("tensor", lambda e: None, inc=[("T", 1)])
        self.emit("vector", lambda e: e.tensor_copy(
            self.eb2R[:, :], self.pC[0:NEXP, :]),
            wait=[("T", self.c("T"))], inc=[("V", 1)])
        self.wait("tensor", ("V", self.c("V")))
        self.dump("R", self.R_sb[:, :, 0:512])
        self.dump("eb2R", self.eb2R[:, 0:1024])
        ar.reset(m0)

    def attn_a(self):
        ar, inp = self.ar, self.inp
        m0 = ar.mark()
        xa = ar.bf([FC, T])
        q_fm = ar.bf([T])
        kvf = self.smx[:, 0:128]
        kex = self.smy[:, 0:64]
        ksm = ar.bf([64])
        vtile = ar.bf([64])
        ctxs = ar.f32([64])
        ctxr = ar.f32([64])
        ctxbd = ar.bf([64])
        oa = ar.bf([512])
        atmp = [ar.f32([512]), ar.f32([512])]
        self.alloc_ln_scratch()

        if KCUT < 1:
            ar.reset(m0); return
        self.wait("tensor", "dX", "dC", "dCg", "G")
        if KCUT < 2:
            ar.reset(m0); return
        for g in range(4):
            sl = slice(g * 512, (g + 1) * 512)
            self.ln_tile(lambda fc, sl=sl: self.stream[:, fc, sl],
                         lambda fc, sl=sl: xa[:, fc, sl], 512)
        wv_ln = self.c("V")
        self.dump("xa", xa[:, :, 0:512])
        for g in range(4):
            sl = slice(g * 512, (g + 1) * 512)
            for fc in range(FC):
                self.emit("tensor", lambda e, fc=fc, sl=sl: e.matmul(
                    self.pA[0:64, :], self.wqa[:, fc, :], xa[:, fc, sl],
                    start=(fc == 0), stop=(fc == FC - 1)),
                    wait=([("V", wv_ln), ("S", self.c("S"))] if fc == 0 else []))
            self.emit("tensor", lambda e: None, inc=[("T", 1)])
            self.emit("scalar", lambda e, sl=sl: e.copy(q_fm[0:64, sl], self.pA[0:64, :]),
                      wait=[("T", self.c("T"))], inc=[("S", 1)])
        self.dump("q", q_fm[0:64, 0:2048], "scalar")
        if KCUT < 3:
            ar.reset(m0); return
        for qc in range(NSLOT):
            sl = slice(qc * 128, (qc + 1) * 128)
            for fc in range(FC):
                self.emit("tensor", lambda e, fc=fc, sl=sl: e.matmul(
                    self.pB[:, 0:128], xa[:, fc, sl], self.wkva[:, fc, :],
                    start=(fc == 0), stop=(fc == FC - 1)),
                    wait=([("V", self.c("V"))] if qc > 0 and fc == 0 else
                          ([("S", self.c("S"))] if fc == 0 else [])))
            self.emit("tensor", lambda e: None, inc=[("T", 1)])
            wt = self.c("T")
            self.emit("vector", lambda e: e.tensor_copy(kvf[:], self.pB[:, 0:128]),
                      wait=[("T", wt)], inc=[("V", 1)])
            nmx = self.smy[:, 64:72]
            sm = self.smy[:, 72:80]
            for h in range(8):
                self.emit("vector", lambda e, h=h: e.reduce_max(
                    nmx[:, h:h + 1], kvf[:, 8 * h:8 * h + 8],
                    axis=mybir.AxisListType.X))
            for h in range(8):
                self.emit("vector", lambda e, h=h: e.tensor_scalar(
                    kvf[:, 8 * h:8 * h + 8], kvf[:, 8 * h:8 * h + 8],
                    nmx[:, h:h + 1], None, ALU.subtract))
            self.emit("vector", lambda e: None, inc=[("V", 1)])
            self.emit("scalar", lambda e: e.activation(
                kex, kvf[:, 0:64], AF.Exp),
                wait=[("V", self.c("V"))], inc=[("S", 1)])
            self.wait("vector", ("S", self.c("S")))
            for h in range(8):
                self.emit("vector", lambda e, h=h: e.reduce_sum(
                    sm[:, h:h + 1], kex[:, 8 * h:8 * h + 8],
                    axis=mybir.AxisListType.X))
            rsm = self.smy[:, 80:88]
            self.emit("vector", lambda e: e.drain())
            self.emit("vector", lambda e: e.reciprocal(rsm, sm[:]))
            self.emit("vector", lambda e: e.drain())
            for h in range(8):
                self.emit("vector", lambda e, h=h: e.tensor_scalar(
                    ksm[:, 8 * h:8 * h + 8], kex[:, 8 * h:8 * h + 8],
                    rsm[:, h:h + 1], None, ALU.mult))
            self.emit("vector", lambda e: e.tensor_copy(vtile[:], kvf[:, 64:128]),
                      inc=[("V", 1)])
            if qc == KQC:
                self.dump("kv", kvf[:])
                self.dump("ksm", ksm[:])
                self.dump("vt", vtile[:])
                self.dump("kex", kex)
            self.emit("tensor", lambda e, qc=qc: e.matmul(
                self.pS[0:64, 0:64], ksm[:], vtile[:],
                start=(qc == 0), stop=(qc == NSLOT - 1)),
                wait=[("V", self.c("V"))])
        self.emit("tensor", lambda e: None, inc=[("T", 1)])
        self.emit("vector", lambda e: e.tensor_copy(ctxs[0:64, :], self.pS[0:64, 0:64]),
                  wait=[("T", self.c("T"))], inc=[("V", 1)])
        self.dump("ctx", ctxs[0:64, :])
        self.emit("gpsimd", lambda e: e.dma_start(out=self.ctx_in, in_=ctxs[0:64, :]),
                  wait=[("V", self.c("V"))], inc=[("dM", 16)])
        self.emit("gpsimd", lambda e: e.collective_compute(
            "AllReduce", ALU.add, replica_groups=GRP4,
            ins=[self.ctx_in.opt()], outs=[self.ctx_out.opt()]),
            wait=[("dM", self.c("dM"))], inc=[("cc", 1)])
        self.emit("gpsimd", lambda e: e.dma_start(out=ctxr[0:64, :], in_=self.ctx_out),
                  wait=[("cc", self.c("cc"))], inc=[("dM", 16)])
        if KCUT < 4:
            ar.reset(m0); return
        self.emit("vector", lambda e: e.tensor_tensor(
            ctxbd[0:64, :], ctxr[0:64, :], self.blkm[:], ALU.mult),
            wait=[("dM", self.c("dM"))], inc=[("V", 1)])
        if KCUT < 5:
            ar.reset(m0); return
        self.dump("ctxbd", ctxbd[0:64, :])
        wv_ctx = self.c("V")
        for g in range(4):
            sl = slice(g * 512, (g + 1) * 512)
            self.emit("tensor", lambda e, sl=sl: e.matmul(
                self.pA[0:64, :], ctxbd[0:64, :], q_fm[0:64, sl],
                start=True, stop=True),
                wait=[("V", wv_ctx)] + ([("V", self.c("V"))] if g > 0 else []),
                inc=[("T", 1)])
            self.emit("scalar", lambda e: e.copy(oa[0:64, :], self.pA[0:64, :]),
                      wait=[("T", self.c("T"))], inc=[("S", 1)])
            ws_oa = self.c("S")
            for cc in range(FC):
                self.emit("tensor", lambda e, cc=cc: e.matmul(
                    self.pB[:], self.wouta[:, cc, :], oa[0:64, :],
                    start=True, stop=True),
                    wait=([("S", ws_oa)] if cc == 0 else []) +
                         ([("V", self.c("V"))] if cc > 0 or g > 0 else []),
                    inc=[("T", 1)])
                at = atmp[cc % 2]
                self.emit("scalar", lambda e, cc=cc, at=at: e.activation(
                    at[:], self.pB[:], AF.Identity,
                    bias=self.c_bout_a[:, cc:cc + 1]),
                    wait=[("T", self.c("T"))], inc=[("S", 1)])
                self.emit("vector", lambda e, cc=cc, sl=sl, at=at: e.tensor_tensor(
                    self.stream[:, cc, sl], self.stream[:, cc, sl], at[:],
                    ALU.add), wait=[("S", self.c("S"))], inc=[("V", 1)])
        self.wait("tensor", "V")
        ar.reset(m0)

    def ffn(self, call):
        ar, inp = self.ar, self.inp
        m0 = ar.mark()
        ln1 = ar.bf([FC, 512])
        ln1_raw = ar.A[:, ar.last_off:ar.last_off + 4096]
        xnr = ar.bf([FC * 512])
        xn = xnr.rearrange("p (a b) -> p a b", a=FC, b=512)
        moe_bf = xn
        r3raw = ar.bf([NHC * 512])
        h_w = r3raw.rearrange("p (a b) -> p a b", a=NHC, b=512)
        rot_f = ar.f32([FC, 512])
        wtok = ar.f32([4, NEXP])
        wfm = ar.bf([512])
        wq_bf = ar.bf([8])
        g_b = ar.f32([512])
        vtmp = ar.f32([512])
        gprod = vtmp.bitcast(BF16)[:, 0:512]
        mt_single = ar.f32([512])
        mixtmp = [mt_single, mt_single]
        ew1R = [ar.bf([4, FC, 128]) for _ in range(2)]
        ew2R = [ar.bf([8, 512]) for _ in range(3)]
        abelw = ar.bf([FC, D])
        self.ln_xb = xn
        self.ln_xq = r3raw[:, 0:4096].rearrange("p (a b) -> p a b", a=FC, b=512)
        self.ln_mu = r3raw[:, 4096:5120].bitcast(F32)
        self.ln_var = r3raw[:, 5120:6144].bitcast(F32)
        self.ln_isg = r3raw[:, 6144:7168].bitcast(F32)
        abel_f = r3raw.bitcast(F32).rearrange(
            "p (c t) -> p c t", c=FC, t=512)

        for eng in ("tensor", "vector", "scalar"):
            self.wait(eng, ("dX", 16), "dC", "dCg", "G")
        self.emit("gpsimd", lambda e: e.dma_start(
            out=abelw[:], in_=inp["abel_w"]),
            wait=[("T", self.c("T")), ("V", self.c("V"))], inc=[("dCg", 16)])
        w_abelw = self.c("dCg")
        if call == 1:
            self.wait("gpsimd", ("dB", self.c("dB")))

        first_pass = (call == 0)
        r_state = {("w1", s): dict(rev=0, tfree=0, wb=0) for s in range(2)}
        r_state.update({("w2", s): dict(rev=0, tfree=0, wb=0) for s in range(3)})
        seq = {"w1": 0, "w2": 0}

        def load_w1(e_i, hg, g_i):
            s = seq["w1"] % 2
            st = r_state[("w1", s)]
            if first_pass and g_i == 0:
                src = inp["ew1"][e_i, hg * 4:(hg + 1) * 4].rearrange(
                    "h p f k -> p h f k")
            else:
                src = self.ewc1[e_i, hg * 4:(hg + 1) * 4].rearrange(
                    "h p f k -> p h f k")
            w = [("T", st["tfree"])] if st["tfree"] else []
            if st["wb"]:
                w.append(("dB", st["wb"]))
            self.emit("gpsimd", lambda e, s=s, src=src: e.dma_start(
                out=ew1R[s][:], in_=src), wait=w, inc=[(f"dW1_{s}", 16)])
            st["rev"] += 1
            if first_pass and g_i == 0:
                self.emit("sync", lambda e, s=s, e_i=e_i, hg=hg: e.dma_start(
                    out=self.ewc1[e_i, hg * 4:(hg + 1) * 4].rearrange(
                        "h p f k -> p h f k"),
                    in_=ew1R[s][:]),
                    wait=[(f"dW1_{s}", 16 * st["rev"])], inc=[("dB", 16)])
                st["wb"] = self.c("dB")
            seq["w1"] += 1
            return s, (f"dW1_{s}", 16 * st["rev"])

        def load_w2(e_i, ch, hh, g_i):
            s = seq["w2"] % 3
            st = r_state[("w2", s)]
            if first_pass and g_i == 0:
                src = inp["ew2"][e_i, hh * 8:(hh + 1) * 8,
                                 :, ch * 512:(ch + 1) * 512].rearrange(
                    "h p c -> p h c")
            else:
                src = self.ewc2[e_i, hh * 8:(hh + 1) * 8,
                                :, ch * 512:(ch + 1) * 512].rearrange(
                    "h p c -> p h c")
            w = [("T", st["tfree"])] if st["tfree"] else []
            if st["wb"]:
                w.append(("dB", st["wb"]))
            self.emit("gpsimd", lambda e, s=s, src=src: e.dma_start(
                out=ew2R[s][:], in_=src), wait=w, inc=[(f"dW2_{s}", 16)])
            st["rev"] += 1
            if first_pass and g_i == 0:
                self.emit("sync", lambda e, s=s, e_i=e_i, ch=ch, hh=hh: e.dma_start(
                    out=self.ewc2[e_i, hh * 8:(hh + 1) * 8,
                                  :, ch * 512:(ch + 1) * 512].rearrange(
                        "h p c -> p h c"),
                    in_=ew2R[s][:]),
                    wait=[(f"dW2_{s}", 16 * st["rev"])], inc=[("dB", 16)])
                st["wb"] = self.c("dB")
            seq["w2"] += 1
            return s, (f"dW2_{s}", 16 * st["rev"])

        for g in range(4):
            gsl = slice(g * 512, (g + 1) * 512)
            self.ln_tile(lambda fc, gsl=gsl: self.stream[:, fc, gsl],
                         lambda fc: ln1[:, fc, :], 512)
            if g == 0:
                self.dump("ln1e", ln1)
            self.emit("sync", lambda e, gsl=gsl: e.dma_start(
                out=self.evst, in_=self.stream[:, :, gsl]),
                wait=[("V", self.c("V"))], inc=[("dSt", 16)])
            evict_done = self.c("dSt")
            if g == 0:
                self.dump("ln1f", ln1)
            self.ln_tile(lambda fc: ln1[:, fc, :], lambda fc: xn[:, fc, :],
                         512, gamma=self.c_ln_g, beta=self.c_ln_b, src_bf=True)
            if g == 0:
                self.dump("ln1g", ln1)
                self.dump("xn0", xn)
            if g == 1:
                self.dump("xn1", xn)
            for qc in range(4):
                qsl = slice(qc * 128, (qc + 1) * 128)
                for fc in range(FC):
                    self.emit("tensor", lambda e, fc=fc, qsl=qsl: e.matmul(
                        self.pS[:, 0:64], xn[:, fc, qsl], self.c_gatew[:, fc, :],
                        start=(fc == 0), stop=(fc == FC - 1)),
                        wait=([("V", self.c("V"))] if fc == 0 else []))
                self.emit("tensor", lambda e: None, inc=[("T", 1)])
                wt = self.c("T")
                sx = self.smy
                lg, m1, nm1 = sx[:, 0:8], sx[:, 8:9], sx[:, 9:10]
                lm, m2, ge2 = sx[:, 10:18], sx[:, 18:19], sx[:, 19:27]
                expl, den, rden = sx[:, 27:35], sx[:, 35:36], sx[:, 36:37]
                if KCUTG < 2:
                    continue
                DR = lambda: self.emit("vector", lambda e: e.drain())
                self.emit("vector", lambda e: e.tensor_tensor(
                    lg, self.pS[:, 0:NEXP], self.c_gb[:], ALU.add),
                    wait=[("T", wt)])
                DR()
                self.emit("vector", lambda e: e.reduce_max(
                    m1, lg, axis=mybir.AxisListType.X))
                DR()
                self.emit("vector", lambda e: e.tensor_scalar(
                    nm1, m1, -1.0, None, ALU.mult))
                DR()
                self.emit("vector", lambda e: e.tensor_scalar(
                    lm, lg, m1[:, 0:1], None, ALU.is_ge))
                DR()
                self.emit("vector", lambda e: e.scalar_tensor_tensor(
                    lm, lm, -1e30, lg, ALU.mult, ALU.add))
                DR()
                self.emit("vector", lambda e: e.reduce_max(
                    m2, lm, axis=mybir.AxisListType.X), inc=[("V", 1)])
                self.emit("scalar", lambda e: e.activation(
                    expl, lg, AF.Exp, bias=nm1[:, 0:1]),
                    wait=[("V", self.c("V"))])
                self.emit("scalar", lambda e: e.activation(
                    den, m2, AF.Exp, bias=nm1[:, 0:1]), inc=[("S", 1)])
                self.emit("vector", lambda e: e.tensor_scalar(
                    den, den, 1.0, None, ALU.add), wait=[("S", self.c("S"))])
                self.emit("vector", lambda e: e.drain())
                self.emit("vector", lambda e: e.reciprocal(rden, den))
                self.emit("vector", lambda e: e.drain())
                self.emit("vector", lambda e: e.tensor_scalar(
                    ge2, lg, m2[:, 0:1], None, ALU.is_ge))
                DR()
                self.emit("vector", lambda e: e.tensor_tensor(
                    ge2, ge2, expl, ALU.mult))
                DR()
                if KCUTG < 3:
                    self.emit("vector", lambda e: None, inc=[("V", 1)])
                    continue
                self.emit("vector", lambda e, qc=qc: e.tensor_scalar(
                    wtok[:, qc, :], ge2, rden[:, 0:1], None, ALU.mult))
                DR()
                if g == 0 and qc == 0:
                    self.dump("smy0", self.smy[:])
                self.emit("vector", lambda e, qc=qc: e.tensor_copy(
                    wq_bf[:], wtok[:, qc, :]), inc=[("V", 1)])
                if KCUTG < 4:
                    continue
                self.emit("tensor", lambda e: e.transpose(
                    self.pT[0:NEXP, 0:128], wq_bf[:], self.ident[:]),
                    wait=[("V", self.c("V"))], inc=[("T", 1)])
                self.emit("vector", lambda e, qc=qc: e.tensor_copy(
                    wfm[0:NEXP, qc * 128:(qc + 1) * 128],
                    self.pT[0:NEXP, 0:128]),
                    wait=[("T", self.c("T"))], inc=[("V", 1)])
            if g == 0:
                self.dump("wtok0", wtok)
            if g == 1:
                self.dump("wtok1", wtok)
            acc_written = False
            for e_i in range(NEXP):
                cur1, rdy1 = load_w1(e_i, 0, g)
                nxt1 = None
                for hg in range(4):
                    if hg < 3:
                        nxt1 = load_w1(e_i, hg + 1, g)
                    for hin in range(4):
                        hc = hg * 4 + hin
                        pu = self.pA if hc % 2 == 0 else self.pB
                        for fc in range(FC):
                            w = []
                            if fc == 0:
                                w.append(("S", self.c("S")))
                                if hin == 0:
                                    w.append(rdy1)
                                if hc == 0:
                                    w.append(("V", self.c("V")))
                            self.emit("tensor", lambda e, fc=fc, hin=hin, pu=pu,
                                      c1=cur1: e.matmul(
                                pu[:], ew1R[c1][:, hin, fc, :], xn[:, fc, :],
                                start=(fc == 0), stop=(fc == FC - 1)), wait=w)
                        self.emit("tensor", lambda e: None, inc=[("T", 1)])
                        self.emit("scalar", lambda e, hc=hc, pu=pu, e_i=e_i:
                                  e.activation(h_w[:, hc, :], pu[:], AF.Gelu,
                                  bias=self.c_eb1[:, e_i * NHC + hc:
                                                  e_i * NHC + hc + 1]),
                                  wait=[("T", self.c("T"))], inc=[("S", 1)])
                    r_state[("w1", cur1)]["tfree"] = self.c("T")
                    if hg < 3:
                        cur1, rdy1 = nxt1
                if g == 0 and e_i == 0:
                    self.dump("hw0", h_w)
                if g == 1 and e_i == 0:
                    self.dump("hw1", h_w)
                ws_up = self.c("S")
                for ch in range(2):
                    curA, rdyA = load_w2(e_i, ch, 0, g)
                    curB, rdyB = load_w2(e_i, ch, 1, g)
                    for qc in range(4):
                        pd = self.pC if qc % 2 == 0 else self.pD
                        for hc in range(NHC):
                            sl = curA if hc < 8 else curB
                            rd = rdyA if hc < 8 else rdyB
                            w = []
                            if hc == 0:
                                w.append(("S", ws_up))
                                w.append(("V", self.c("V")))
                                w.append(rdyA)
                            if hc == 8:
                                w.append(rdyB)
                            self.emit("tensor", lambda e, hc=hc, sl=sl, pd=pd,
                                      qc=qc: e.matmul(
                                pd[:, 0:512], h_w[:, hc, qc * 128:(qc + 1) * 128],
                                ew2R[sl][:, hc % 8, :],
                                start=(hc == 0), stop=(hc == NHC - 1)), wait=w)
                        self.emit("tensor", lambda e: None, inc=[("T", 1)])
                        wt = self.c("T")
                        if g == 1 and e_i == 0 and ch == 0 and qc == 0:
                            self.dump("w2s", ew2R[curA][:])
                        dst = self.stream[:, 2 * qc + ch, gsl]
                        wap = wtok[:, qc, e_i:e_i + 1]
                        if not acc_written and e_i == 0:
                            self.emit("vector", lambda e, dst=dst, wap=wap, pd=pd:
                                      e.tensor_scalar(dst, pd[:, 0:512], wap,
                                                      None, ALU.mult),
                                      wait=[("T", wt), ("dSt", evict_done)],
                                      inc=[("V", 1)])
                        else:
                            self.emit("vector", lambda e, dst=dst, wap=wap, pd=pd:
                                      e.scalar_tensor_tensor(dst, pd[:, 0:512],
                                                             wap, dst,
                                                             ALU.mult, ALU.add),
                                      wait=[("T", wt)], inc=[("V", 1)])
                    r_state[("w2", curA)]["tfree"] = self.c("T")
                    r_state[("w2", curB)]["tfree"] = self.c("T")
            acc_written = True
            if g == 0:
                self.dump("acc0", self.stream[:, :, gsl])
            if g == 1:
                self.dump("acc1", self.stream[:, :, gsl])
            for cc in range(FC):
                pu = self.pA if cc % 2 == 0 else self.pB
                for fc in range(FC):
                    w = []
                    if fc == 0:
                        w.append(("S", self.c("S")))
                        if cc == 0:
                            w.append(("dCg", w_abelw))
                            w.append(("V", self.c("V")))
                    self.emit("tensor", lambda e, fc=fc, cc=cc, pu=pu: e.matmul(
                        pu[:], abelw[:, fc, cc * 128:(cc + 1) * 128],
                        xn[:, fc, :], start=(fc == 0), stop=(fc == FC - 1)),
                        wait=w)
                self.emit("tensor", lambda e: None, inc=[("T", 1)])
                self.emit("scalar", lambda e, cc=cc, pu=pu: e.activation(
                    abel_f[:, cc, :], pu[:], AF.Gelu,
                    bias=self.c_abel_b[:, cc:cc + 1]),
                    wait=[("T", self.c("T"))], inc=[("S", 1)])
            ws_abel = self.c("S")
            for cc in range(FC):
                for qc in range(4):
                    acc_sl = self.stream[:, 2 * qc + cc // 4, gsl][
                        :, (cc % 4) * 128:(cc % 4) * 128 + 128]
                    w = [("V", self.c("V"))]
                    if cc == 0 and qc == 0:
                        w.append(("S", ws_abel))
                    self.emit("tensor", lambda e, qc=qc, acc_sl=acc_sl:
                              e.transpose(self.pC[:, qc * 128:(qc + 1) * 128],
                                          acc_sl, self.ident_f[:]), wait=w,
                              inc=[("T", 1)])
                self.emit("vector", lambda e, cc=cc: e.tensor_copy(
                    moe_bf[:, cc, :], self.pC[:, 0:512]),
                    wait=[("T", self.c("T"))], inc=[("V", 1)])
            for half in range(2):
                pr = self.pC if half == 0 else self.pD
                for j in range(4):
                    cpc = half * 4 + j
                    for cc in range(FC):
                        w = []
                        if cc == 0 and j == 0:
                            w.append(("V", self.c("V")))
                        if cc == 0 and j >= 2:
                            w.append(("S", self.c("S")))
                        self.emit("tensor", lambda e, cc=cc, cpc=cpc, j=j, pr=pr:
                                  e.matmul(
                            pr[:, (j % 2) * 512:(j % 2) * 512 + 512],
                            self.R_sb[:, cc, cpc * 128:(cpc + 1) * 128],
                            moe_bf[:, cc, :],
                            start=(cc == 0), stop=False), wait=w)
                    self.emit("tensor", lambda e, cpc=cpc, j=j, pr=pr: e.matmul(
                        pr[:, (j % 2) * 512:(j % 2) * 512 + 512],
                        self.eb2R[:, cpc * 128:(cpc + 1) * 128],
                        wfm[0:NEXP, :], start=False, stop=True))
                    self.emit("tensor", lambda e: None, inc=[("T", 1)])
                    self.emit("scalar", lambda e, cpc=cpc, j=j, pr=pr: e.copy(
                        rot_f[:, cpc, :], pr[:, (j % 2) * 512:(j % 2) * 512 + 512]),
                        wait=[("T", self.c("T"))], inc=[("S", 1)])
            if g == 0:
                self.dump("abel0", abel_f)
                self.dump("rot0", rot_f)
                self.dump("moe0", moe_bf)
            if g == 1:
                self.dump("abel1", abel_f)
                self.dump("rot1", rot_f)
                self.dump("moe1", moe_bf)
            ws_rot = self.c("S")
            first = True
            for si, (src_f, gw) in enumerate(((abel_f, self.c_gcw1),
                                              (rot_f, self.c_gcw2))):
                for cc in range(FC):
                    w = [("S", ws_rot)] if first else [("T", self.c("T"))]
                    self.emit("vector", lambda e, src_f=src_f, gw=gw, cc=cc:
                              e.tensor_scalar(gprod[:], src_f[:, cc, :],
                                              gw[:, cc:cc + 1], None, ALU.mult),
                              wait=w, inc=[("V", 1)])
                    stop = (si == 1 and cc == FC - 1)
                    self.emit("tensor", lambda e, first=first, stop=stop: e.matmul(
                        self.pA[:], self.ones_all[:], gprod[:],
                        start=first, stop=stop),
                        wait=[("V", self.c("V"))], inc=[("T", 1)])
                    first = False
            self.emit("scalar", lambda e: e.activation(
                g_b[:], self.pA[:], AF.Sigmoid, bias=self.c_gcb[:, 0:1]),
                wait=[("T", self.c("T"))], inc=[("S", 1)])
            self.wait("vector", ("S", self.c("S")))
            if g == 0:
                self.dump("gb0", g_b)
            if g == 1:
                self.dump("gb1", g_b)
            for cc in range(FC):
                mt = mixtmp[cc % 2]
                self.emit("sync", lambda e, cc=cc, mt=mt: e.dma_start(
                    out=mt[:], in_=self.evst[:, cc, :]),
                    wait=([("V", self.c("V"))] if cc >= 1
                          else [("dSt", evict_done)]),
                    inc=[("dSt", 16)])
                rst = self.c("dSt")
                self.emit("vector", lambda e, cc=cc: e.tensor_tensor(
                    vtmp[:], rot_f[:, cc, :], abel_f[:, cc, :], ALU.subtract))
                self.emit("vector", lambda e: e.tensor_tensor(
                    vtmp[:], vtmp[:], g_b[:], ALU.mult))
                self.emit("vector", lambda e, cc=cc: e.tensor_tensor(
                    vtmp[:], vtmp[:], abel_f[:, cc, :], ALU.add))
                self.emit("vector", lambda e, cc=cc: e.tensor_tensor(
                    vtmp[:], vtmp[:], ln1[:, cc, :], ALU.add))
                self.emit("vector", lambda e, cc=cc, mt=mt, gsl=gsl:
                          e.tensor_tensor(self.stream[:, cc, gsl], vtmp[:],
                                          mt[:], ALU.add),
                          wait=[("dSt", rst)], inc=[("V", 1)])
            if g == 1:
                self.dump("mix1", self.stream[:, :, gsl])
                self.dump("ln1m1", ln1)
            if KCUT_FFN <= g:
                break
        self.wait("tensor", ("V", self.c("V")))
        self.wait("sync", ("V", self.c("V")))
        ar.reset(m0)

    def attn_b(self):
        ar, inp = self.ar, self.inp
        m0 = ar.mark()
        q_fm = ar.bf([FC, T])
        b_sb = ar.f32([NSLOT])
        scr = ar.bf([1024])
        w_sb = scr[:, 0:512]
        wT_sb = scr[:, 512:1024]
        rs_sb = ar.f32([8, NSLOT])
        mA = ar.mark()

        wq = ar.bf([FC, D])
        wk = ar.bf([FC, D])
        wv = ar.bf([FC, D])
        xn_b = ar.bf([FC, 512])
        kf = ar.bf([FC, 512])
        vtk = ar.bf([D])
        qsq = self.smx[:].bitcast(BF16)[:, 0:128]
        self.ln_xb = xn_b
        self.ln_xq = kf
        self.ln_mu = ar.f32([512])
        self.ln_var = ar.f32([512])
        self.ln_isg = ar.f32([512])

        wA = [("T", self.c("T")), ("V", self.c("V")), ("S", self.c("S"))]
        for dst, s in ((wq, "wq_b"), (wk, "wk_b"), (wv, "wv_b")):
            self.emit("gpsimd", lambda e, d=dst, s=s: e.dma_start(
                out=d[:], in_=inp[s]), wait=wA, inc=[("dCg", 16)])
            wA = []
        w_wqkv = self.c("dCg")

        kf_dma = 0
        vtk_dma = 0
        for g in range(4):
            gsl = slice(g * 512, (g + 1) * 512)
            if kf_dma:
                self.wait("scalar", ("dM", kf_dma))
            self.ln_tile(lambda fc, gsl=gsl: self.stream[:, fc, gsl],
                         lambda fc: xn_b[:, fc, :], 512)
            wv_ln = self.c("V")
            for wgt, is_q in ((wq, True), (wk, False)):
                for co in range(FC):
                    pu = self.pA if co % 2 == 0 else self.pB
                    for ci in range(FC):
                        w = []
                        if ci == 0:
                            w = [("S", self.c("S")), ("V", wv_ln)]
                            if co == 0 and is_q:
                                w.append(("dCg", w_wqkv))
                        self.emit("tensor", lambda e, wgt=wgt, ci=ci, co=co,
                                  pu=pu: e.matmul(
                            pu[:, 0:512], wgt[:, ci, co * 128:(co + 1) * 128],
                            xn_b[:, ci, :], start=(ci == 0),
                            stop=(ci == FC - 1)), wait=w)
                    self.emit("tensor", lambda e: None, inc=[("T", 1)])
                    if is_q:
                        dst = q_fm[:, co, gsl]
                        bc = self.c_bq_b
                    else:
                        dst = kf[:, co, :]
                        bc = self.c_bk_b
                    w = [("T", self.c("T"))]
                    if (not is_q) and co == 0 and kf_dma:
                        w.append(("dM", kf_dma))
                    self.emit("scalar", lambda e, dst=dst, bc=bc, co=co,
                              pu=pu: e.activation(
                        dst, pu[:, 0:512], AF.Identity,
                        bias=bc[:, co:co + 1]), wait=w, inc=[("S", 1)])
            self.emit("sync", lambda e, g=g: e.dma_start(
                out=self.kdr[4 * g:4 * g + 4].rearrange("s p a t -> p a s t"),
                in_=kf.rearrange("p a (s t) -> p a s t", s=4, t=128)),
                wait=[("S", self.c("S"))], inc=[("dM", 16)])
            kf_dma = self.c("dM")
            for s4 in range(4):
                sl = slice(s4 * 128, (s4 + 1) * 128)
                for ch in range(2):
                    pu = self.pA if ch == 0 else self.pB
                    for ci in range(FC):
                        w = []
                        if ci == 0:
                            w = [("S", self.c("S")), ("V", self.c("V"))]
                        self.emit("tensor", lambda e, ci=ci, ch=ch, sl=sl,
                                  pu=pu: e.matmul(
                            pu[:, 0:512], xn_b[:, ci, sl],
                            wv[:, ci, ch * 512:(ch + 1) * 512],
                            start=(ci == 0), stop=(ci == FC - 1)), wait=w)
                    self.emit("tensor", lambda e: None, inc=[("T", 1)])
                    w = [("T", self.c("T"))]
                    if ch == 0 and vtk_dma:
                        w.append(("dM", vtk_dma))
                    self.emit("vector", lambda e, ch=ch, pu=pu: e.tensor_copy(
                        vtk[:, ch * 512:(ch + 1) * 512], pu[:, 0:512]),
                        wait=w, inc=[("V", 1)])
                self.emit("sync", lambda e, g=g, s4=s4: e.dma_start(
                    out=self.vdr[4 * g + s4], in_=vtk[:]),
                    wait=[("V", self.c("V"))], inc=[("dM", 16)])
                vtk_dma = self.c("dM")
        for i in range(NSLOT):
            qsl = slice(i * 128, (i + 1) * 128)
            for cc in range(FC):
                w = [("T", self.c("T"))] if cc else [("S", self.c("S"))]
                self.emit("vector", lambda e, cc=cc, qsl=qsl: e.tensor_tensor(
                    qsq[:], q_fm[:, cc, qsl], q_fm[:, cc, qsl], ALU.mult),
                    wait=w, inc=[("V", 1)])
                w = [("V", self.c("V"))]
                if cc == 0:
                    w.append(("S", self.c("S")))
                self.emit("tensor", lambda e, cc=cc: e.matmul(
                    self.pS[:, 0:64], qsq[:], self.ones_all[:, 0:64],
                    start=(cc == 0), stop=(cc == FC - 1)),
                    wait=w, inc=[("T", 1)])
            self.emit("scalar", lambda e, i=i: e.activation(
                b_sb[:, i:i + 1], self.pS[:, 0:1], AF.Identity,
                bias=0.0, scale=-1.0 / 64.0),
                wait=[("T", self.c("T"))], inc=[("S", 1)])
            self.emit("scalar", lambda e: e.drain())
        self.dump("bsb", b_sb)
        self.dump("qfmb", q_fm[:, :, 0:1024])

        self.emit("gpsimd", lambda e: e.collective_compute(
            "AllGather", ALU.bypass, replica_groups=GRP4,
            ins=[self.kdr.opt()], outs=[self.kg.opt()]),
            wait=[("dM", self.c("dM"))], inc=[("cc", 1)])
        self.emit("gpsimd", lambda e: e.collective_compute(
            "AllGather", ALU.bypass, replica_groups=GRP4,
            ins=[self.vdr.opt()], outs=[self.vg.opt()]),
            inc=[("cc", 1)])
        w_gath = self.c("cc")

        ar.reset(mA)
        PV = ar.f32([8, D])
        woT = ar.bf([FC, D])
        kbuf = ar.bf([FC, 512])
        vbuf = ar.bf([4, D])
        pvT = ar.bf([FC, 512])
        otmp = ar.f32([512])
        sc_f = otmp
        bv64 = ar.bf([FC, 64])
        obias = ar.f32([FC])
        rsum = self.smy[:, 88:89]
        rinv = self.smy[:, 89:90]

        self.emit("gpsimd", lambda e: e.dma_start(out=woT[:], in_=inp["wo_b"]),
                  wait=[("T", self.c("T")), ("V", self.c("V")),
                        ("S", self.c("S"))], inc=[("dCg", 16)])
        w_wo = self.c("dCg")
        for cb in range(FC):
            self.emit("vector", lambda e, cb=cb: e.tensor_scalar(
                bv64[:, cb, :], self.ones_all[:, 0:64],
                self.c_bv_fm[:, cb:cb + 1], None, ALU.mult), inc=[("V", 1)])
        for co in range(FC):
            for cb in range(FC):
                w = []
                if cb == 0:
                    w = [("V", self.c("V")), ("S", self.c("S")),
                         ("dCg", w_wo)]
                self.emit("tensor", lambda e, cb=cb, co=co: e.matmul(
                    self.pS[:, 64:128], woT[:, cb, co * 128:(co + 1) * 128],
                    bv64[:, cb, :], start=(cb == 0), stop=(cb == FC - 1)),
                    wait=w)
            self.emit("tensor", lambda e: None, inc=[("T", 1)])
            self.emit("scalar", lambda e, co=co: e.activation(
                obias[:, co:co + 1], self.pS[:, 64:65], AF.Identity,
                bias=self.c_bo_b[:, co:co + 1]),
                wait=[("T", self.c("T"))], inc=[("S", 1)])
            self.emit("scalar", lambda e: e.drain())

        tfree_kv = 0
        for qh in range(2):
            for kb in range(8 * qh + 8):
                self.emit("gpsimd", lambda e, kb=kb: e.dma_start(
                    out=kbuf.rearrange("p a (c t) -> p a c t", c=4, t=128),
                    in_=self.kg[:, kb].rearrange("c p a t -> p a c t")),
                    wait=[("cc", w_gath), ("T", tfree_kv)],
                    inc=[("dK0", 16)])
                self.emit("gpsimd", lambda e, kb=kb: e.dma_start(
                    out=vbuf[:],
                    in_=self.vg[:, kb].rearrange("c t d -> t c d")),
                    inc=[("dV0", 16)])
                wk_rdy = self.c("dK0")
                wv_rdy = self.c("dV0")
                slots = [s for s in range(8 * qh, 8 * qh + 8) if s >= kb]
                sct = {}
                exp_s = {}

                def emit_scores(n, i):
                    pu = self.pA if n % 2 == 0 else self.pB
                    for cc in range(FC):
                        w = []
                        if cc == 0:
                            w = [("S", self.c("S")), ("dK0", wk_rdy)]
                        self.emit("tensor", lambda e, cc=cc, i=i, pu=pu:
                                  e.matmul(
                            pu[:, 0:512],
                            q_fm[:, cc, i * 128:(i + 1) * 128], kbuf[:, cc, :],
                            start=(cc == 0), stop=(cc == FC - 1)), wait=w)
                    self.emit("tensor", lambda e: None, inc=[("T", 1)])
                    sct[n] = self.c("T")

                def emit_exp(n, i, li):
                    pu = self.pA if n % 2 == 0 else self.pB
                    acc = rs_sb[:, li, kb:kb + 1]
                    if i == kb:
                        self.emit("vector", lambda e, pu=pu: e.tensor_tensor(
                            sc_f[:], pu[:, 0:512], self.cmask[:], ALU.add),
                            wait=[("T", self.c("T")), ("V", self.c("V"))],
                            inc=[("V", 1)])
                        self.emit("scalar", lambda e, i=i, acc=acc:
                                  e.activation(
                            w_sb[:], sc_f[:], AF.Exp,
                            bias=b_sb[:, i:i + 1], scale=1.0 / 32.0,
                            accum_out=acc),
                            wait=[("V", self.c("V"))], inc=[("S", 1)])
                    else:
                        self.emit("scalar", lambda e, i=i, acc=acc, pu=pu:
                                  e.activation(
                            w_sb[:], pu[:, 0:512], AF.Exp,
                            bias=b_sb[:, i:i + 1], scale=1.0 / 32.0,
                            accum_out=acc),
                            wait=[("T", self.c("T"))], inc=[("S", 1)])

                def emit_tail(n, i, li):
                    for tc in range(4):
                        w = [("S", self.c("S")),
                             ("V", self.c("V"))] if tc == 0 else []
                        self.emit("tensor", lambda e, tc=tc: e.transpose(
                            self.pT[:, tc * 128:(tc + 1) * 128],
                            w_sb[:, tc * 128:(tc + 1) * 128], self.ident[:]),
                            wait=w)
                    self.emit("tensor", lambda e: None, inc=[("T", 1)])
                    self.emit("vector", lambda e: e.tensor_copy(
                        wT_sb[:], self.pT[:, 0:512]),
                        wait=[("T", self.c("T"))], inc=[("V", 1)])
                    pv = self.pC if n % 2 == 0 else self.pD
                    wTv = wT_sb.rearrange("p (a b) -> p a b", a=4, b=128)
                    for ch in range(2):
                        for kc in range(4):
                            w = []
                            if kc == 0:
                                w = [("V", self.c("V"))]
                                if ch == 0:
                                    w.append(("dV0", wv_rdy))
                            self.emit("tensor", lambda e, ch=ch, kc=kc, pv=pv,
                                      wTv=wTv: e.matmul(
                                pv[:, ch * 512:(ch + 1) * 512], wTv[:, kc, :],
                                vbuf[:, kc, ch * 512:(ch + 1) * 512],
                                start=(kc == 0), stop=(kc == 3)), wait=w)
                    self.emit("tensor", lambda e: None, inc=[("T", 1)])
                    if kb == 0:
                        self.emit("vector", lambda e, li=li, pv=pv:
                                  e.tensor_copy(PV[:, li, :], pv[:, 0:1024]),
                                  wait=[("T", self.c("T"))], inc=[("V", 1)])
                    else:
                        self.emit("vector", lambda e, li=li, pv=pv:
                                  e.tensor_tensor(PV[:, li, :], PV[:, li, :],
                                                  pv[:, 0:1024], ALU.add),
                                  wait=[("T", self.c("T"))], inc=[("V", 1)])
                    if i == kb:
                        emit_finalize(i, li)

                def emit_finalize(i, li):
                    self.emit("vector", lambda e, li=li, i=i: e.reduce_sum(
                        rsum, rs_sb[:, li, 0:i + 1], axis=mybir.AxisListType.X),
                        wait=[("S", self.c("S"))])
                    self.emit("vector", lambda e: e.drain())
                    self.emit("vector", lambda e: e.reciprocal(rinv, rsum))
                    self.emit("vector", lambda e: e.drain())
                    self.emit("vector", lambda e, li=li: e.tensor_scalar(
                        scr[:, 0:1024], PV[:, li, :], rinv[:, 0:1], None,
                        ALU.mult), inc=[("V", 1)])
                    sc = i % 4
                    for r in range(2):
                        for tc in range(4):
                            cb = 4 * r + tc
                            w = [("V", self.c("V"))] if tc == 0 else []
                            self.emit("tensor", lambda e, tc=tc, cb=cb:
                                      e.transpose(
                                self.pT[:, tc * 128:(tc + 1) * 128],
                                scr[:, cb * 128:(cb + 1) * 128],
                                self.ident[:]), wait=w)
                        self.emit("tensor", lambda e: None, inc=[("T", 1)])
                        self.emit("vector", lambda e, r=r, sc=sc: e.tensor_copy(
                            pvT[:, 4 * r:4 * r + 4, sc * 128:(sc + 1) * 128],
                            self.pT[:, 0:512].rearrange(
                                "p (a b) -> p a b", a=4, b=128)),
                            wait=[("T", self.c("T"))], inc=[("V", 1)])
                    if i % 4 == 3:
                        emit_outproj(i // 4)

                def emit_outproj(gq):
                    gsl = slice(gq * 512, (gq + 1) * 512)
                    for co in range(FC):
                        pu = self.pA if co % 2 == 0 else self.pB
                        for cb in range(FC):
                            w = []
                            if cb == 0:
                                w = [("S", self.c("S")), ("V", self.c("V")),
                                     ("dCg", w_wo)]
                            self.emit("tensor", lambda e, cb=cb, co=co, pu=pu:
                                      e.matmul(
                                pu[:, 0:512],
                                woT[:, cb, co * 128:(co + 1) * 128],
                                pvT[:, cb, :], start=(cb == 0),
                                stop=(cb == FC - 1)), wait=w)
                        self.emit("tensor", lambda e: None, inc=[("T", 1)])
                        self.emit("scalar", lambda e, co=co, pu=pu:
                                  e.activation(
                            otmp[:], pu[:, 0:512], AF.Identity,
                            bias=obias[:, co:co + 1]),
                            wait=[("T", self.c("T")), ("V", self.c("V"))],
                            inc=[("S", 1)])
                        self.emit("vector", lambda e, co=co, gsl=gsl:
                                  e.tensor_tensor(
                            self.stream[:, co, gsl], self.stream[:, co, gsl],
                            otmp[:], ALU.add),
                            wait=[("S", self.c("S"))], inc=[("V", 1)])

                for n, i in enumerate(slots):
                    emit_scores(n, i)
                    if n > 0:
                        emit_exp(n - 1, slots[n - 1], slots[n - 1] - 8 * qh)
                        emit_tail(n - 1, slots[n - 1], slots[n - 1] - 8 * qh)
                nl = len(slots) - 1
                emit_exp(nl, slots[nl], slots[nl] - 8 * qh)
                emit_tail(nl, slots[nl], slots[nl] - 8 * qh)
                tfree_kv = self.c("T")
        self.wait("sync", ("V", self.c("V")))
        ar.reset(m0)

    def epilogue(self):
        self.wait("sync", "V", "S", "T", "G", "dX", "dC", "dCg")
        self.emit("sync", lambda e: e.dma_start(out=self.out_fm, in_=self.stream[:]),
                  inc=[("dE", 16)])
        self.wait("sync", "dE")



def _core_slices(x):
    outs = []
    for c in range(NCORE):
        b, j = divmod(c, 4)
        xr = x[b].reshape(64, 128, FC, 128)[j::4]
        outs.append(np.ascontiguousarray(
            xr.transpose(3, 2, 0, 1).reshape(128, FC, T)))
    return outs


def _unslice(cores):
    x = np.empty((NB, SEQ, D), np.float32)
    for c in range(NCORE):
        b, j = divmod(c, 4)
        xr = cores[c].reshape(128, FC, NSLOT, 128).transpose(2, 3, 1, 0)
        x[b].reshape(64, 128, FC, 128)[j::4] = xr
    return x


def _pmaj(vec):
    return np.ascontiguousarray(np.asarray(vec, np.float32).reshape(FC, 128).T)


def _wmaj(w):
    w = np.asarray(w, np.float32)
    n = w.shape[1]
    return np.ascontiguousarray(w.reshape(FC, 128, n).transpose(1, 0, 2))


def _padgw(gw):
    out = np.zeros((128, FC, 64), np.float32)
    out[:, :, 0:NEXP] = gw.reshape(FC, 128, NEXP).transpose(1, 0, 2)
    return out


def _prep_inputs(ins):
    f = np.float32
    a = lambda v: np.asarray(v, f)
    Wqkv = a(ins["L0_attn_a_Wqkv"])
    gens = a(ins["L0_ffn_gens"])
    com = {
        "wq_a": _wmaj(Wqkv[:, 0:64]),
        "wkv_a": _wmaj(Wqkv[:, 64:192]),
        "wout_a": np.ascontiguousarray(
            a(ins["L0_attn_a_Wout"]).reshape(64, FC, 128)),
        "bout_a": _pmaj(ins["L0_attn_a_bout"]),
        "wq_b": _wmaj(a(ins["L0_attn_b_Wq"]).T),
        "wk_b": _wmaj(a(ins["L0_attn_b_Wk"]).T),
        "wv_b": _wmaj(a(ins["L0_attn_b_Wv"]).T),
        "wo_b": _wmaj(a(ins["L0_attn_b_Wo"]).T),
        "bq_b": _pmaj(ins["L0_attn_b_bq"]),
        "bk_b": _pmaj(ins["L0_attn_b_bk"]),
        "bv_fm": _pmaj(ins["L0_attn_b_bv"]),
        "bo_b": _pmaj(ins["L0_attn_b_bo"]),
        "ln_g": _pmaj(ins["L0_ffn_ln_gamma"]),
        "ln_b": _pmaj(ins["L0_ffn_ln_beta"]),
        "abel_w": _wmaj(ins["L0_ffn_abel_w"]),
        "abel_b": _pmaj(ins["L0_ffn_abel_b"]),
        "gate_w": _padgw(a(ins["L0_ffn_moe_gate_w"])),
        "gate_b": np.ascontiguousarray(np.broadcast_to(
            a(ins["L0_ffn_moe_gate_b"]).reshape(1, NEXP), (128, NEXP))),
        "ew1": np.ascontiguousarray(
            a(ins["L0_ffn_ew1"]).reshape(NEXP, FC, 128, NHC, 128)
            .transpose(0, 3, 2, 1, 4)),
        "eb1": np.ascontiguousarray(
            a(ins["L0_ffn_eb1"]).reshape(NEXP, NHC, 128).transpose(2, 0, 1)
            .reshape(128, NEXP * NHC)),
        "ew2": np.ascontiguousarray(
            a(ins["L0_ffn_ew2"]).reshape(NEXP, NHC, 128, D)),
        "eb2T": np.ascontiguousarray(
            a(ins["L0_ffn_eb2"]).T.reshape(FC, 128, NEXP).transpose(1, 0, 2)),
        "gcw1": _pmaj(ins["L0_ffn_gc_w"][0:D, 0]),
        "gcw2": _pmaj(ins["L0_ffn_gc_w"][D:2 * D, 0]),
        "gcb": np.ascontiguousarray(np.broadcast_to(
            a(ins["L0_ffn_gc_b"]).reshape(1, 1), (128, 1))),
        "gens": np.ascontiguousarray(
            gens.reshape(3, FC, 128, D).transpose(0, 2, 1, 3)),
    }
    xs = _core_slices(a(ins["L0_attn_a_x"]))
    maps = []
    for c in range(NCORE):
        j = c % 4
        m = dict(com)
        m["x_fm"] = xs[c]
        m["gensT"] = np.ascontiguousarray(
            gens[:, c * 128:(c + 1) * 128, :].transpose(0, 2, 1)
            .reshape(3, FC, 128, 128).transpose(0, 2, 1, 3))
        m["diag_off"] = np.full((128, 1), float(j * 128), f)
        ei = np.zeros((128, FC), f)
        ei[:, c] = 1.0
        m["eye_ind"] = ei
        maps.append(m)
    return maps


def kernel(_phases=ALL_PHASES, **inputs):
    maps = _prep_inputs(inputs)
    nc = K(_phases).build()
    res = run_bass_kernel_spmd(nc, maps, core_ids=list(range(NCORE)))
    return _unslice([np.asarray(r["out_fm"]) for r in res.results])
